# version 12
# baseline (speedup 1.0000x reference)
import numpy as np
import concourse.bass as bass
import concourse.mybir as mybir
from concourse.bass_utils import run_bass_kernel_spmd

F32 = mybir.dt.float32
BF16 = mybir.dt.bfloat16
I32 = mybir.dt.int32
U8 = mybir.dt.uint8
AF = mybir.ActivationFunctionType
ALU = mybir.AluOpType
AX = mybir.AxisListType

T = 1024
TT = 256
NTILE = T // TT
D = 2048
NST = 1024
K1_127 = float(np.float32(1.0 / np.sqrt(2048.0)) / np.float32(127.0))
K2_127 = float(np.float32(1.0 / np.sqrt(8192.0)) / np.float32(127.0))
E1 = float(np.float32(1e-5) * np.float32(np.sqrt(2048.0)))
E2C = float(np.float32(1e-5) * np.float32(np.sqrt(8192.0)))
WMEAN = 1.0 / 16777216.0

ENGINES = ("sync", "vector", "scalar", "tensor", "gpsimd")
SAME_ENGINE_SYNC = True


def _hadamard(n):
    h = np.array([[1.0]], dtype=np.float32)
    while h.shape[0] < n:
        h = np.block([[h, h], [h, -h]]).astype(np.float32)
    return h


class Arena:
    def __init__(self, nc, nbytes):
        self.nc = nc
        self.handle = nc.alloc_sbuf_tensor("arena", [128, nbytes], U8)
        self.base = nc.lookup_mloc(self.handle).addr

    def at(self, name, shape, dtype, off):
        nc = self.nc
        h = nc.virtual_sbuf_tensor(name, list(shape), dtype)
        nc._tensor(h.name, h.shape, h.dtype, type="SB")
        mloc = nc.lookup_mloc(h)
        mloc.allocated = not nc.target_bir_lowering
        mloc.addr = self.base + off
        return h


class Lay:
    def __init__(self, base):
        self.off = base

    def take(self, nbytes):
        o = (self.off + 31) // 32 * 32
        self.off = o + nbytes
        return o


class Sch:

    def __init__(self):
        self.ops = []
        self.last_w = {}
        self.readers = {}
        self.bar = set()

    def barrier(self):
        tails = {}
        for op in self.ops:
            tails[op["eng"]] = op
        self.bar = set()
        for op in tails.values():
            self.bar.add(id(op))
            self._byid[id(op)] = op

    def _deps(self, reads, writes):
        deps = set(self.bar)
        for t in reads:
            w = self.last_w.get(t)
            if w is not None:
                deps.add(id(w)); self._byid[id(w)] = w
        for t in writes:
            w = self.last_w.get(t)
            if w is not None:
                deps.add(id(w)); self._byid[id(w)] = w
            for r in self.readers.get(t, ()):
                deps.add(id(r)); self._byid[id(r)] = r
        return deps

    _byid = {}

    def _commit(self, op, reads, writes):
        for t in reads:
            self.readers.setdefault(t, []).append(op)
        for t in writes:
            self.last_w[t] = op
            self.readers[t] = []
        self.ops.append(op)

    def op(self, eng, emit, reads=(), writes=()):
        o = {"kind": "c", "eng": eng, "emit": emit,
             "deps": self._deps(reads, writes)}
        self._commit(o, reads, writes)
        return o

    def dma(self, eng, dst, src, reads=(), writes=()):
        o = {"kind": "d", "eng": eng, "dst": dst, "src": src,
             "deps": self._deps(reads, writes)}
        self._commit(o, reads, writes)
        return o

    def cc(self, emit, reads=(), writes=()):
        o = {"kind": "cc", "eng": "gpsimd", "emit": emit,
             "deps": self._deps(reads, writes)}
        self._commit(o, reads, writes)
        return o

    def emit(self, nc, block, sems):
        prog, dq, ccsem = sems
        qs = {e: [] for e in ENGINES}
        for op in self.ops:
            qs[op["eng"]].append(op)
        for e in ENGINES:
            dn = 0
            for op in qs[e]:
                if op["kind"] == "d":
                    op["dno"] = dn
                    dn += 1
        byid = Sch._byid
        for op in self.ops:
            for did in op["deps"]:
                d = byid[did]
                if d["kind"] == "d":
                    continue
                if d["eng"] != op["eng"]:
                    d["needs_inc"] = True
                elif SAME_ENGINE_SYNC and op["eng"] != "tensor":
                    d["needs_inc"] = True
        for e in ENGINES:
            m = 0
            for op in qs[e]:
                if op.get("needs_inc"):
                    m += 1
                    op["mno"] = m

        def mkbody(e):
            def body(eng):
                waited = {}
                for op in qs[e]:
                    wants = {}
                    for did in op["deps"]:
                        d = byid[did]
                        if d["kind"] == "d":
                            key = ("dq", d["eng"])
                            val = 16 * (d["dno"] + 1)
                        else:
                            if d["eng"] == e and not d.get("needs_inc"):
                                continue
                            if d["eng"] == e and not (SAME_ENGINE_SYNC and e != "tensor"):
                                continue
                            key = ("pg", d["eng"])
                            val = d["mno"]
                        if wants.get(key, 0) < val:
                            wants[key] = val
                    for key in sorted(wants, key=str):
                        val = wants[key]
                        if waited.get(key, 0) >= val:
                            continue
                        sem = dq[key[1]] if key[0] == "dq" else prog[key[1]]
                        eng.wait_ge(sem, val)
                        waited[key] = val
                    if op["kind"] == "c":
                        inst = op["emit"](eng)
                        if op.get("needs_inc"):
                            inst.then_inc(prog[e], 1)
                    elif op["kind"] == "d":
                        eng.dma_start(out=op["dst"], in_=op["src"]).then_inc(dq[e], 16)
                    else:
                        op["emit"](eng).then_inc(ccsem, 1)
                        eng.wait_ge(ccsem, 1)
                        if op.get("needs_inc"):
                            eng.nop().then_inc(prog[e], 1)
            return body

        for e in ENGINES:
            getattr(block, e)(mkbody(e))


def build(debug=False):
    nc = bass.Bass("TRN2", target_bir_lowering=False)
    SCR = "ExternalOutput" if debug else "Internal"

    xsh = nc.dram_tensor("xsh", [T, D], F32, kind="ExternalInput").ap()
    bTd = nc.dram_tensor("bT", [D, NST], F32, kind="ExternalInput").ap()
    cTd = nc.dram_tensor("cT", [NST, D], F32, kind="ExternalInput").ap()
    dcolh = nc.dram_tensor("dcolh", [128, 16], F32, kind="ExternalInput").ap()
    wgb = nc.dram_tensor("wgb", [64, 128, 2048], F32, kind="ExternalInput").ap()
    wub = nc.dram_tensor("wub", [64, 128, 2048], F32, kind="ExternalInput").ap()
    wdb = nc.dram_tensor("wdb", [4, 16, 128, 2048], F32, kind="ExternalInput").ap()
    wsl = nc.dram_tensor("wsl", [3, 128, 16384], F32, kind="ExternalInput").ap()
    ysh = nc.dram_tensor("ysh", [T, D], F32, kind="ExternalOutput").ap()

    hsc = nc.dram_tensor("hsc", [64, 128, T], F32, kind=SCR).ap()
    q2sc = nc.dram_tensor("q2sc", [NTILE, 128, 64 * TT], BF16, kind=SCR).ap()
    accsc = nc.dram_tensor("accsc", [T, D], F32, kind=SCR).ap()
    ffnsc = nc.dram_tensor("ffnsc", [16, 128, T], F32, kind=SCR).ap()
    if debug:
        dbgq1 = nc.dram_tensor("dbgq1", [128, 16 * T], BF16, kind="ExternalOutput").ap()
        dbgrow = nc.dram_tensor("dbgrow", [4, T], F32, kind="ExternalOutput").ap()
    ccin = nc.dram_tensor("ccin", [1, 4], F32, kind="Internal").ap()
    ccout = nc.dram_tensor("ccout", [1, 4], F32, kind="Internal").ap()

    h128c = nc.inline_tensor(_hadamard(128), name="h128c").ap()
    s2c = nc.inline_tensor(np.kron(_hadamard(64), np.eye(2, dtype=np.float32)).astype(np.float32),
                           name="s2c").ap()
    idenc = nc.inline_tensor(np.eye(128, dtype=np.float32), name="idenc").ap()
    onesrc = nc.inline_tensor(np.ones((1, 128), np.float32), name="onesrc").ap()
    onescc = nc.inline_tensor(np.ones((128, 1), np.float32), name="onescc").ap()

    AR = Arena(nc, 205824)
    P = Lay(0)
    hs = AR.at("hs", [128, 128], F32, P.take(512))
    s2s = AR.at("s2s", [128, 128], F32, P.take(512))
    ident = AR.at("ident", [128, 128], F32, P.take(512))
    ones_s = AR.at("ones_s", [1, 128], F32, P.take(512))
    onescol = AR.at("onescol", [128, 1], F32, P.take(32))
    dcol = AR.at("dcol", [128, 16], F32, P.take(64))
    rwb = AR.at("rwb", [128, 4], F32, P.take(32))
    scals = AR.at("scals", [1, 16], F32, P.take(64))
    colsc = AR.at("colsc", [128, 4], F32, P.take(16))
    rowtmp = AR.at("rowtmp", [2, 128], F32, P.take(512))
    m1row = AR.at("m1row", [1, T], F32, P.take(4096))
    brow = AR.at("brow", [1, T], F32, P.take(4096))
    rbrow = AR.at("rbrow", [1, T], F32, P.take(4096))
    grow = AR.at("grow", [1, T], F32, P.take(4096))
    rowsc = AR.at("rowsc", [1, 1024], F32, P.take(4096))
    bnb = AR.at("bnb", [128, 24], F32, P.take(96))
    bn2 = AR.at("bn2", [128, 2], F32, P.take(8))
    lnc = AR.at("lnc", [128, 8], F32, P.take(32))
    ccst = AR.at("ccst", [1, 4], F32, P.take(16))
    cols24 = AR.at("cols24", [128, 24], F32, P.take(96))
    col3 = AR.at("col3", [128, 4], F32, P.take(16))
    OB = (P.off + 31) // 32 * 32

    Q1_OFF = OB
    BT_OFF = OB + 32768
    CT_OFF = BT_OFF + 32768
    PRE_END = CT_OFF + 32768

    q1 = AR.at("q1", [128, 16 * T], BF16, Q1_OFF)
    BTb = AR.at("BTb", [128, 16 * NST], BF16, BT_OFF)
    CTb = AR.at("CTb", [128, 8 * D], BF16, CT_OFF)

    L0 = Lay(OB)
    wtmp0 = AR.at("wtmp0", [128, 2048], F32, L0.take(8192))
    wtmp1 = AR.at("wtmp1", [128, 2048], F32, L0.take(8192))
    wtmps = (wtmp0, wtmp1)

    L1 = Lay(PRE_END)
    xTM = AR.at("xTM", [128, 2 * D], F32, L1.take(16384))
    lnst = AR.at("lnst", [128, 2048], F32, L1.take(8192))
    xnF = AR.at("xnF", [128, 16 * TT], F32, L1.take(16384))
    Yf = AR.at("Yf", [128, 16 * TT], F32, L1.take(16384))
    xnB = AR.at("xnB", [128, 16 * TT], BF16, L1.take(8192))
    tanhS = AR.at("tanhS", [128, 8 * TT], BF16, L1.take(4096))
    accst = AR.at("accst", [128, D], F32, L1.take(8192))
    intst = AR.at("intst", [128, 512], I32, L1.take(2048))
    s1bc = AR.at("s1bc", [128, TT], F32, L1.take(1024))
    Mb = AR.at("Mb", [128, TT], F32, L1.take(1024))

    L15 = Lay(OB + 32768)
    wgraw = AR.at("wgraw", [128, 2 * 2048], F32, L15.take(16384))
    wuraw = AR.at("wuraw", [128, 2 * 2048], F32, L15.take(16384))
    wgq = AR.at("wgq", [128, 2 * 2048], BF16, L15.take(8192))
    wuq = AR.at("wuq", [128, 2 * 2048], BF16, L15.take(8192))
    wint = AR.at("wint", [128, 2048], I32, L15.take(8192))
    abc = AR.at("abc", [128, T], F32, L15.take(4096))
    tmpg = AR.at("tmpg", [128, T], F32, L15.take(4096))
    tmps = AR.at("tmps", [128, T], F32, L15.take(4096))
    hst = AR.at("hst", [128, T], F32, L15.take(4096))
    hstF = AR.at("hstF", [128, T], F32, L15.take(4096))

    L2 = Lay(OB)
    ping = AR.at("ping", [128, 64 * TT], F32, L2.take(65536))
    pong = AR.at("pong", [128, 64 * TT], F32, L2.take(65536))
    Mb2 = AR.at("Mb2", [128, TT], F32, L2.take(1024))
    s2bc = AR.at("s2bc", [128, TT], F32, L2.take(1024))
    q2st = AR.at("q2st", [128, 2 * 2048], BF16, L2.take(8192))
    i2st = AR.at("i2st", [128, 2 * 2048], I32, L2.take(16384))

    L3 = Lay(OB)
    q2res = AR.at("q2res", [128, 64 * T], BF16, L3.take(131072))
    wdraw = AR.at("wdraw", [128, 2 * 2048], F32, L3.take(16384))
    wdq = AR.at("wdq", [128, 2 * 2048], BF16, L3.take(8192))
    wdint = AR.at("wdint", [128, 2048], I32, L3.take(8192))
    gbc = AR.at("gbc", [128, T], F32, L3.take(4096))
    ffnst = AR.at("ffnst", [128, 2 * T], F32, L3.take(8192))

    L4 = Lay(OB)
    accrd = AR.at("accrd", [128, 2 * D], F32, L4.take(16384))
    ffrd = AR.at("ffrd", [128, 2 * D], F32, L4.take(16384))
    ystage = AR.at("ystage", [128, 2 * D], F32, L4.take(16384))

    psA = nc.alloc_psum_tensor("psA", [128, 2048], F32)
    psB = nc.alloc_psum_tensor("psB", [128, 2048], F32)

    S = Sch()
    CONST = ("const",)

    S.dma("sync", hs[:, :], h128c, writes=[CONST])
    S.dma("sync", s2s[:, :], s2c, writes=[CONST])
    S.dma("sync", ident[:, :], idenc, writes=[CONST])
    S.dma("sync", ones_s[:, :], onesrc, writes=[CONST])
    S.dma("sync", onescol[:, :], onescc, writes=[CONST])
    S.dma("sync", dcol[:, :], dcolh, writes=[CONST])
    S.op("vector", lambda v: v.memset(scals[0:1, :], 0.0), writes=[("scals",)])
    S.op("vector", lambda v: v.memset(ccst[0:1, :], 0.0), writes=[("ccst",)])
    for m in range(3):
        for ch in range(8):
            i = m * 8 + ch
            par = i % 2
            S.dma("sync", wtmps[par][:, :], wsl[m, :, ch * 2048:(ch + 1) * 2048],
                  writes=[("wtmp", par)])
            S.op("vector", lambda v, i=i, par=par: v.tensor_reduce(
                cols24[:, i:i + 1], wtmps[par][:, :], AX.X, ALU.add,
                apply_absolute_value=True), reads=[("wtmp", par)], writes=[("c24", i)])
    for m in range(3):
        S.op("vector", lambda v, m=m: v.reduce_sum(
            col3[:, m:m + 1], cols24[:, m * 8:(m + 1) * 8], AX.X),
            reads=[("c24", m * 8 + j) for j in range(8)], writes=[("col3",)])
    S.op("tensor", lambda t: t.matmul(psA[0:1, 0:3], onescol[:, 0:1], col3[:, 0:3],
                                      start=True, stop=True),
         reads=[("col3",), CONST], writes=[("psA",)])
    S.op("vector", lambda v: v.tensor_copy(ccst[0:1, 0:3], psA[0:1, 0:3]),
         reads=[("psA",)], writes=[("ccst",)])
    S.dma("gpsimd", ccin, ccst[0:1, :], reads=[("ccst",)], writes=[("ccin",)])
    S.cc(lambda g: g.collective_compute(
        "AllReduce", ALU.add, replica_groups=[list(range(8))],
        ins=[ccin.opt()], outs=[ccout.opt()]),
        reads=[("ccin",)], writes=[("ccout",)])
    S.dma("gpsimd", ccst[0:1, :], ccout, reads=[("ccout",)], writes=[("ccst",)])
    S.op("vector", lambda v: v.tensor_scalar(scals[0:1, 0:3], ccst[0:1, 0:3],
                                             WMEAN, 1e-5, ALU.mult, ALU.max),
         reads=[("ccst",)], writes=[("scals",)])
    S.op("vector", lambda v: v.reciprocal(scals[0:1, 4:7], scals[0:1, 0:3]),
         reads=[("scals",)], writes=[("scals",)])
    S.op("vector", lambda v: v.tensor_single_scalar(scals[0:1, 8:9], scals[0:1, 0:1], K1_127, ALU.mult),
         reads=[("scals",)], writes=[("scals",)])
    S.op("vector", lambda v: v.tensor_single_scalar(scals[0:1, 9:10], scals[0:1, 1:2], K1_127, ALU.mult),
         reads=[("scals",)], writes=[("scals",)])
    S.op("vector", lambda v: v.tensor_single_scalar(scals[0:1, 10:11], scals[0:1, 2:3], K2_127, ALU.mult),
         reads=[("scals",)], writes=[("scals",)])
    S.op("tensor", lambda t: t.matmul(psA[0:128, 0:4], ones_s[0:1, 0:128], scals[0:1, 4:8],
                                      start=True, stop=True),
         reads=[("scals",), CONST], writes=[("psA",)])
    S.op("scalar", lambda s: s.copy(rwb[:, 0:4], psA[:, 0:4]),
         reads=[("psA",)], writes=[("rwb",)])
    for dc in range(16):
        par = dc % 2
        S.dma("sync", wtmps[par][:, 0:1024], bTd[dc * 128:(dc + 1) * 128, :], writes=[("wtmp", par)])
        S.op("vector", lambda v, dc=dc, par=par: v.tensor_copy(
            BTb[:, dc * 1024:(dc + 1) * 1024], wtmps[par][:, 0:1024]),
            reads=[("wtmp", par)], writes=[("BTb",)])
    for ncc in range(8):
        par = ncc % 2
        S.dma("sync", wtmps[par][:, :], cTd[ncc * 128:(ncc + 1) * 128, :], writes=[("wtmp", par)])
        S.op("vector", lambda v, ncc=ncc, par=par: v.tensor_copy(
            CTb[:, ncc * 2048:(ncc + 1) * 2048], wtmps[par][:, :]),
            reads=[("wtmp", par)], writes=[("CTb",)])

    xnFv = xnF[:, :].rearrange("p (c t) -> p c t", c=16)
    xnBv = xnB[:, :].rearrange("p (c t) -> p c t", c=16)
    psAv = psA[:, :].rearrange("p (c u) -> p c u", c=16)
    for tile in range(NTILE):
        t0 = tile * TT
        for g in range(2):
            S.dma("scalar", xTM[:, g * D:(g + 1) * D], xsh[t0 + g * 128:t0 + (g + 1) * 128, :],
                  writes=[("xTM", g)])
        for g in range(2):
            xg = xTM[:, g * D:(g + 1) * D]
            for i in range(4):
                S.op("vector", lambda v, xg=xg, i=i: v.bn_stats(
                    bnb[:, i * 6:(i + 1) * 6], xg[:, i * 512:(i + 1) * 512]),
                    reads=[("xTM", g)], writes=[("bnb",)])
            S.op("vector", lambda v: v.bn_aggr(bn2[:, 0:2], bnb[:, 0:24]),
                 reads=[("bnb",)], writes=[("bn2",)])
            S.op("vector", lambda v: v.tensor_single_scalar(lnc[:, 0:1], bn2[:, 1:2], 1e-5, ALU.add),
                 reads=[("bn2",)], writes=[("lnc",)])
            S.op("scalar", lambda s: s.activation(lnc[:, 1:2], lnc[:, 0:1], AF.Sqrt),
                 reads=[("lnc",)], writes=[("lnc",)])
            S.op("vector", lambda v: v.reciprocal(lnc[:, 2:3], lnc[:, 1:2]),
                 reads=[("lnc",)], writes=[("lnc",)])
            S.op("vector", lambda v: v.tensor_mul(lnc[:, 3:4], lnc[:, 2:3], lnc[:, 2:3]),
                 reads=[("lnc",)], writes=[("lnc",)])
            S.op("vector", lambda v: v.tensor_mul(lnc[:, 3:4], lnc[:, 3:4], lnc[:, 0:1]),
                 reads=[("lnc",)], writes=[("lnc",)])
            S.op("vector", lambda v: v.tensor_scalar(lnc[:, 3:4], lnc[:, 3:4], -0.5, 1.5, ALU.mult, ALU.add),
                 reads=[("lnc",)], writes=[("lnc",)])
            S.op("vector", lambda v: v.tensor_mul(lnc[:, 2:3], lnc[:, 2:3], lnc[:, 3:4]),
                 reads=[("lnc",)], writes=[("lnc",)])
            S.op("vector", lambda v: v.scalar_tensor_tensor(
                lnc[:, 4:5], bn2[:, 0:1], -1.0, lnc[:, 2:3], ALU.mult, ALU.mult),
                reads=[("lnc",), ("bn2",)], writes=[("lnc",)])
            psg = psA if g == 0 else psB
            ptokg = ("psA",) if g == 0 else ("psB",)
            psgv = psg[:, :].rearrange("p (c u) -> p c u", c=16)
            S.op("scalar", lambda s, xg=xg: s.activation(
                lnst[:, :], xg, AF.Identity, bias=lnc[:, 4:5], scale=lnc[:, 2:3]),
                reads=[("xTM", g), ("lnc",)], writes=[("lnst",)])
            for dc in range(16):
                S.op("tensor", lambda t, dc=dc, psg=psg: t.transpose(
                    psg[:, dc * 128:(dc + 1) * 128], lnst[:, dc * 128:(dc + 1) * 128], ident[:, :]),
                    reads=[("lnst",), CONST], writes=[ptokg])
            S.op("vector", lambda v, g=g, psgv=psgv: v.tensor_copy(
                xnFv[:, :, g * 128:(g + 1) * 128], psgv),
                reads=[ptokg], writes=[("xnF",)])
            S.op("scalar", lambda s, g=g: s.copy(
                xnBv[:, :, g * 128:(g + 1) * 128], xnFv[:, :, g * 128:(g + 1) * 128]),
                reads=[("xnF",)], writes=[("xnB",)])
        for ncc in range(8):
            for dc in range(16):
                S.op("tensor", lambda t, ncc=ncc, dc=dc: t.matmul(
                    psA[:, ncc * TT:(ncc + 1) * TT],
                    BTb[:, dc * 1024 + ncc * 128: dc * 1024 + (ncc + 1) * 128],
                    xnB[:, dc * TT:(dc + 1) * TT],
                    start=(dc == 0), stop=(dc == 15)),
                    reads=[("xnB",), ("BTb",)], writes=[("psA",)])
        S.op("scalar", lambda s: s.activation(tanhS[:, :], psA[:, :], AF.Tanh),
             reads=[("psA",)], writes=[("tanhS",)])
        for wv in range(2):
            psw = psB if wv == 0 else psA
            ptokw = ("psB",) if wv == 0 else ("psA",)
            for dc in range(wv * 8, wv * 8 + 8):
                for ncc in range(8):
                    S.op("tensor", lambda t, dc=dc, ncc=ncc, psw=psw: t.matmul(
                        psw[:, (dc % 8) * TT:((dc % 8) + 1) * TT],
                        CTb[:, ncc * 2048 + dc * 128: ncc * 2048 + (dc + 1) * 128],
                        tanhS[:, ncc * TT:(ncc + 1) * TT],
                        start=(ncc == 0), stop=(ncc == 7)),
                        reads=[("tanhS",), ("CTb",)], writes=[ptokw])
            for dc in range(wv * 8, wv * 8 + 8):
                S.op("vector", lambda v, dc=dc, psw=psw: v.scalar_tensor_tensor(
                    Yf[:, dc * TT:(dc + 1) * TT], xnF[:, dc * TT:(dc + 1) * TT],
                    dcol[:, dc:dc + 1], psw[:, (dc % 8) * TT:((dc % 8) + 1) * TT],
                    ALU.mult, ALU.add),
                    reads=[("xnF",), ptokw, CONST], writes=[("Yf",)])
        for g in range(2):
            for dc in range(16):
                S.op("tensor", lambda t, dc=dc, g=g: t.transpose(
                    psA[:, dc * 128:(dc + 1) * 128],
                    Yf[:, dc * TT + g * 128: dc * TT + (g + 1) * 128], ident[:, :]),
                    reads=[("Yf",), CONST], writes=[("psA",)])
            S.op("vector", lambda v, g=g: v.tensor_add(
                accst[:, :], psA[:, :], xTM[:, g * D:(g + 1) * D]),
                reads=[("psA",), ("xTM", g)], writes=[("accst",)])
            S.dma("gpsimd", accsc[t0 + g * 128:t0 + (g + 1) * 128, :], accst[:, :],
                  reads=[("accst",)], writes=[("accsc", tile, g)])
        for wv in range(2):
            psw = psB if wv == 0 else psA
            ptokw = ("psB",) if wv == 0 else ("psA",)
            for c in range(wv * 8, wv * 8 + 8):
                S.op("tensor", lambda t, c=c, psw=psw: t.matmul(
                    psw[:, (c % 8) * TT:((c % 8) + 1) * TT], hs[:, :],
                    xnF[:, c * TT:(c + 1) * TT], start=True, stop=True),
                    reads=[("xnF",), CONST], writes=[ptokw])
            S.op("scalar", lambda s, wv=wv, psw=psw: s.copy(
                Yf[:, wv * 2048:(wv + 1) * 2048], psw[:, :]),
                reads=[ptokw], writes=[("Yf",)])
        bufs = [(Yf, ("Yf",)), (xnF, ("xnF",))]
        for st in range(4):
            h = 1 << st
            src, stok = bufs[st % 2]
            dst, dtok = bufs[(st + 1) % 2]
            va = src[:, :].rearrange("p (a two rest) -> p a two rest", two=2, rest=h * TT)
            vb = dst[:, :].rearrange("p (a two rest) -> p a two rest", two=2, rest=h * TT)
            S.op("vector", lambda v, va=va, vb=vb: v.tensor_tensor(
                vb[:, :, 0, :], va[:, :, 0, :], va[:, :, 1, :], ALU.add),
                reads=[stok], writes=[dtok])
            S.op("vector", lambda v, va=va, vb=vb: v.tensor_tensor(
                vb[:, :, 1, :], va[:, :, 0, :], va[:, :, 1, :], ALU.subtract),
                reads=[stok], writes=[dtok])
        Yred = Yf[:, :].rearrange("p (c t) -> p t c", c=16)
        S.op("vector", lambda v, Yred=Yred: v.reduce_max(
            Mb[:, 0:TT][:, :, None], Yred, AX.X, apply_absolute_value=True),
            reads=[("Yf",)], writes=[("Mb",)])
        for g in range(2):
            S.op("tensor", lambda t, g=g: t.transpose(
                psB[:, g * 128:(g + 1) * 128], Mb[:, g * 128:(g + 1) * 128], ident[:, :]),
                reads=[("Mb",), CONST], writes=[("psB",)])
        pv = psB[:, 0:256].rearrange("p (g f) -> p g f", g=2)
        S.op("vector", lambda v, pv=pv: v.reduce_max(colsc[:, 0:2][:, :, None], pv, AX.X),
             reads=[("psB",)], writes=[("colsc",)])
        S.op("vector", lambda v: v.tensor_single_scalar(colsc[:, 0:2], colsc[:, 0:2], E1, ALU.max),
             reads=[("colsc",)], writes=[("colsc",)])
        S.op("tensor", lambda t: t.transpose(psB[0:2, 256:384], colsc[:, 0:2], ident[:, :]),
             reads=[("colsc",), CONST], writes=[("psB",)])
        S.op("vector", lambda v: v.tensor_copy(rowtmp[0:2, :], psB[0:2, 256:384]),
             reads=[("psB",)], writes=[("rowtmp",)])
        S.op("vector", lambda v, t0=t0: v.tensor_copy(m1row[0:1, t0:t0 + 128], rowtmp[0:1, :]),
             reads=[("rowtmp",)], writes=[("m1row",)])
        S.dma("sync", m1row[0:1, t0 + 128:t0 + 256], rowtmp[1:2, :],
              reads=[("rowtmp",)], writes=[("m1row",)])
        S.op("vector", lambda v, t0=t0: v.reciprocal(rowsc[0:1, 0:TT], m1row[0:1, t0:t0 + TT]),
             reads=[("m1row",)], writes=[("rowsc",)])
        S.op("vector", lambda v: v.tensor_single_scalar(rowsc[0:1, 256:256 + TT],
                                                        rowsc[0:1, 0:TT], 127.0, ALU.mult),
             reads=[("rowsc",)], writes=[("rowsc",)])
        S.op("tensor", lambda t: t.matmul(psB[0:128, 0:TT], ones_s[0:1, 0:128],
                                          rowsc[0:1, 256:256 + TT], start=True, stop=True),
             reads=[("rowsc",), CONST], writes=[("psB",)])
        S.op("scalar", lambda s: s.copy(s1bc[:, :], psB[:, 0:TT]),
             reads=[("psB",)], writes=[("s1bc",)])
        for c in range(16):
            par = c % 2
            S.op("vector", lambda v, c=c, par=par: v.tensor_tensor(
                intst[:, par * TT:(par + 1) * TT], Yf[:, c * TT:(c + 1) * TT], s1bc[:, :], ALU.mult),
                reads=[("Yf",), ("s1bc",)], writes=[("intst", par)])
            extra = [("wtmp", 0), ("wtmp", 1)] if (tile == 0 and c == 0) else []
            S.op("vector", lambda v, c=c, par=par, t0=t0: v.tensor_copy(
                q1[:, c * T + t0:c * T + t0 + TT], intst[:, par * TT:(par + 1) * TT]),
                reads=[("intst", par)], writes=[("q1",)] + extra)

    S.barrier()
    S.op("vector", lambda v: v.tensor_scalar(rowsc[0:1, 0:1024], m1row[0:1, :],
                                             scals[0:1, 8:9], None, ALU.mult),
         reads=[("m1row",), ("scals",)], writes=[("rowsc",)])
    S.op("vector", lambda v: v.tensor_scalar(brow[0:1, :], m1row[0:1, :],
                                             scals[0:1, 9:10], None, ALU.mult),
         reads=[("m1row",), ("scals",)], writes=[("brow",)])
    for hh in range(2):
        S.op("tensor", lambda t, hh=hh: t.matmul(
            psA[0:128, hh * 512:(hh + 1) * 512], ones_s[0:1, 0:128],
            rowsc[0:1, hh * 512:(hh + 1) * 512], start=True, stop=True),
            reads=[("rowsc",), CONST], writes=[("psA",)])
    S.op("scalar", lambda s: s.copy(abc[:, :], psA[:, 0:1024]),
         reads=[("psA",)], writes=[("abc",)])

    def p15_load(mc):
        par = mc % 2
        S.dma("sync", wgraw[:, par * 2048:(par + 1) * 2048], wgb[mc], writes=[("wgraw", par)])
        S.dma("sync", wuraw[:, par * 2048:(par + 1) * 2048], wub[mc], writes=[("wuraw", par)])

    def p15_tern(mc):
        par = mc % 2
        for nm, raw, qq, ri in (("g", wgraw, wgq, 0), ("u", wuraw, wuq, 1)):
            rv = raw[:, par * 2048:(par + 1) * 2048]
            qv = qq[:, par * 2048:(par + 1) * 2048]
            rtok = ("w%sraw" % nm, par)
            qtok = ("w%sq" % nm, par)
            S.op("vector", lambda v, rv=rv, ri=ri: v.tensor_scalar(
                wint[:, :], rv, rwb[:, ri:ri + 1], -1.49, ALU.mult, ALU.max),
                reads=[rtok, ("rwb",)], writes=[("wint",)])
            S.op("vector", lambda v, qv=qv: v.tensor_scalar(
                qv, wint[:, :], 1.0, None, ALU.min),
                reads=[("wint",)], writes=[qtok])

    def p15_G(mc):
        par = mc % 2
        ps = psA if par == 0 else psB
        gtok = ("psG", par)
        for kc in range(16):
            for th in range(2):
                S.op("tensor", lambda t, kc=kc, th=th, ps=ps, par=par: t.matmul(
                    ps[:, th * 512:(th + 1) * 512],
                    wgq[:, par * 2048 + kc * 128:par * 2048 + (kc + 1) * 128],
                    q1[:, kc * T + th * 512:kc * T + (th + 1) * 512],
                    start=(kc == 0), stop=(kc == 15)),
                    reads=[("wgq", par), ("q1",)], writes=[gtok])

    def p15_U(mc):
        par = mc % 2
        ps = psA if par == 0 else psB
        utok = ("psU", par)
        for kc in range(16):
            for th in range(2):
                S.op("tensor", lambda t, kc=kc, th=th, ps=ps, par=par: t.matmul(
                    ps[:, 1024 + th * 512:1024 + (th + 1) * 512],
                    wuq[:, par * 2048 + kc * 128:par * 2048 + (kc + 1) * 128],
                    q1[:, kc * T + th * 512:kc * T + (th + 1) * 512],
                    start=(kc == 0), stop=(kc == 15)),
                    reads=[("wuq", par), ("q1",)], writes=[utok])

    def p15_epi(mc):
        par = mc % 2
        ps = psA if par == 0 else psB
        gtok = ("psG", par)
        utok = ("psU", par)
        S.op("vector", lambda v, ps=ps: v.tensor_mul(tmpg[:, :], ps[:, 0:1024], abc[:, :]),
             reads=[gtok, ("abc",)], writes=[("tmpg",)])
        S.op("scalar", lambda s: s.activation(tmps[:, :], tmpg[:, :], AF.Silu),
             reads=[("tmpg",)], writes=[("tmps",)])
        S.op("vector", lambda v, ps=ps: v.tensor_mul(hst[:, :], tmps[:, :], ps[:, 1024:2048]),
             reads=[("tmps",), utok], writes=[("hst",)])

    def p15_mmA(mc):
        par = mc % 2
        ps = psA if par == 0 else psB
        gtok = ("psG", par)
        for th in range(2):
            S.op("tensor", lambda t, th=th, ps=ps: t.matmul(
                ps[:, th * 512:(th + 1) * 512], hs[:, :],
                hst[:, th * 512:(th + 1) * 512], start=True, stop=True),
                reads=[("hst",), CONST], writes=[gtok])
        S.op("scalar", lambda s, ps=ps: s.copy(hstF[:, :], ps[:, 0:1024]),
             reads=[gtok], writes=[("hstF",)])
        S.dma("gpsimd", hsc[mc], hstF[:, :], reads=[("hstF",)], writes=[("hsc", mc)])

    for i in range(67):
        if i < 64:
            p15_load(i)
        if 1 <= i <= 64:
            p15_tern(i - 1)
        if 2 <= i <= 65:
            mc = i - 2
            p15_G(mc)
            if mc >= 1:
                p15_mmA(mc - 1)
            p15_U(mc)
            p15_epi(mc)
        if i == 66:
            p15_mmA(63)

    if debug:
        S.dma("sync", dbgq1, q1[:, :], reads=[("q1",)], writes=[("dbgq1",)])

    S.barrier()
    S.op("vector", lambda v: v.reciprocal(rbrow[0:1, :], brow[0:1, :]),
         reads=[("brow",)], writes=[("rbrow",)])
    allhsc = [("hsc", c) for c in range(64)]
    hscr = hsc.rearrange("c (b g) t -> (c b) g t", b=2)
    for tile in range(NTILE):
        t0 = tile * TT
        pv3 = ping[:, :].rearrange("p (c t) -> p c t", c=64)
        for k in range(4):
            S.dma("sync", pv3[:, 16 * k:16 * (k + 1), :],
                  hscr[:, 16 * k:16 * (k + 1), t0:t0 + TT],
                  reads=allhsc, writes=[("ping",)])
        for wv in range(16):
            half = wv % 2
            htok = ("psAh", half)
            for j in range(4):
                gm = wv * 4 + j
                S.op("tensor", lambda t, gm=gm, j=j, half=half: t.matmul(
                    psA[:, half * 1024 + j * TT:half * 1024 + (j + 1) * TT], s2s[:, :],
                    ping[:, gm * TT:(gm + 1) * TT], start=True, stop=True),
                    reads=[("ping",), CONST], writes=[htok])
            S.op("scalar", lambda s, wv=wv, half=half: s.copy(
                pong[:, wv * 1024:(wv + 1) * 1024], psA[:, half * 1024:(half + 1) * 1024]),
                reads=[htok], writes=[("pong", wv)])
        allpong = [("pong", wv) for wv in range(16)]
        Pred = pong[:, :].rearrange("p (c t) -> p t c", c=64)
        S.op("vector", lambda v, Pred=Pred: v.reduce_max(
            Mb2[:, 0:TT][:, :, None], Pred, AX.X, apply_absolute_value=True),
            reads=allpong, writes=[("Mb2",)])
        for g in range(2):
            S.op("tensor", lambda t, g=g: t.transpose(
                psB[:, g * 128:(g + 1) * 128], Mb2[:, g * 128:(g + 1) * 128], ident[:, :]),
                reads=[("Mb2",), CONST], writes=[("psB",)])
        pv = psB[:, 0:256].rearrange("p (g f) -> p g f", g=2)
        S.op("vector", lambda v, pv=pv: v.reduce_max(colsc[:, 0:2][:, :, None], pv, AX.X),
             reads=[("psB",)], writes=[("colsc",)])
        S.op("tensor", lambda t: t.transpose(psB[0:2, 256:384], colsc[:, 0:2], ident[:, :]),
             reads=[("colsc",), CONST], writes=[("psB",)])
        S.op("vector", lambda v: v.tensor_copy(rowtmp[0:2, :], psB[0:2, 256:384]),
             reads=[("psB",)], writes=[("rowtmp",)])
        S.op("vector", lambda v: v.tensor_copy(rowsc[0:1, 0:128], rowtmp[0:1, :]),
             reads=[("rowtmp",)], writes=[("rowsc",)])
        S.dma("sync", rowsc[0:1, 128:256], rowtmp[1:2, :],
              reads=[("rowtmp",)], writes=[("rowsc",)])
        S.op("vector", lambda v, t0=t0: v.scalar_tensor_tensor(
            rowsc[0:1, 256:256 + TT], rbrow[0:1, t0:t0 + TT], E2C,
            rowsc[0:1, 0:TT], ALU.mult, ALU.max),
            reads=[("rowsc",), ("rbrow",)], writes=[("rowsc",)])
        S.op("vector", lambda v: v.reciprocal(rowsc[0:1, 512:512 + TT], rowsc[0:1, 256:256 + TT]),
             reads=[("rowsc",)], writes=[("rowsc",)])
        S.op("vector", lambda v: v.tensor_single_scalar(
            rowsc[0:1, 768:768 + TT], rowsc[0:1, 512:512 + TT], 127.0, ALU.mult),
            reads=[("rowsc",)], writes=[("rowsc",)])
        S.op("vector", lambda v, t0=t0: v.scalar_tensor_tensor(
            grow[0:1, t0:t0 + TT], brow[0:1, t0:t0 + TT], scals[0:1, 10:11],
            rowsc[0:1, 256:256 + TT], ALU.mult, ALU.mult),
            reads=[("brow",), ("scals",), ("rowsc",)], writes=[("grow",)])
        S.op("tensor", lambda t: t.matmul(psB[0:128, 0:TT], ones_s[0:1, 0:128],
                                          rowsc[0:1, 768:768 + TT], start=True, stop=True),
             reads=[("rowsc",), CONST], writes=[("psB",)])
        S.op("scalar", lambda s: s.copy(s2bc[:, :], psB[:, 0:TT]),
             reads=[("psB",)], writes=[("s2bc",)])
        for grp in range(8):
            par = grp % 2
            pview = pong[:, :].rearrange("p (c t) -> p c t", c=64)[:, grp * 8:(grp + 1) * 8, :]
            sview = s2bc[:, None, :].broadcast_to((128, 8, TT))
            iview = i2st[:, par * 2048:(par + 1) * 2048].rearrange("p (c t) -> p c t", c=8)
            S.op("vector", lambda v, pview=pview, sview=sview, iview=iview: v.tensor_tensor(
                iview, pview, sview, ALU.mult),
                reads=allpong + [("s2bc",)], writes=[("i2st", par)])
            S.op("vector", lambda v, par=par: v.tensor_copy(
                q2st[:, par * 2048:(par + 1) * 2048], i2st[:, par * 2048:(par + 1) * 2048]),
                reads=[("i2st", par)], writes=[("q2st", par)])
            S.dma("gpsimd", q2sc[tile, :, grp * 2048:(grp + 1) * 2048],
                  q2st[:, par * 2048:(par + 1) * 2048],
                  reads=[("q2st", par)], writes=[("q2sc", tile, grp)])

    S.barrier()
    for hh in range(2):
        S.op("tensor", lambda t, hh=hh: t.matmul(
            psA[0:128, hh * 512:(hh + 1) * 512], ones_s[0:1, 0:128],
            grow[0:1, hh * 512:(hh + 1) * 512], start=True, stop=True),
            reads=[("grow",), CONST], writes=[("psA",)])
    S.op("scalar", lambda s: s.copy(gbc[:, :], psA[:, 0:1024]),
         reads=[("psA",)], writes=[("gbc",)])
    q2v = q2res[:, :].rearrange("p (c tk) -> p c tk", c=64)
    for tl in range(NTILE):
        src = q2sc[tl].rearrange("p (c t) -> p c t", c=64)
        S.dma("gpsimd", q2v[:, :, tl * TT:(tl + 1) * TT], src,
              reads=[("q2sc", tl, grp) for grp in range(8)], writes=[("q2res",)])

    def p3_load(i):
        dblk, jcq = divmod(i, 16)
        par = i % 2
        S.dma("sync", wdraw[:, par * 2048:(par + 1) * 2048], wdb[dblk, jcq], writes=[("wdraw", par)])

    def p3_tern(i):
        par = i % 2
        rv = wdraw[:, par * 2048:(par + 1) * 2048]
        qv = wdq[:, par * 2048:(par + 1) * 2048]
        S.op("vector", lambda v, rv=rv: v.tensor_scalar(
            wdint[:, :], rv, rwb[:, 2:3], -1.49, ALU.mult, ALU.max),
            reads=[("wdraw", par), ("rwb",)], writes=[("wdint",)])
        S.op("vector", lambda v, qv=qv: v.tensor_scalar(
            qv, wdint[:, :], 1.0, None, ALU.min),
            reads=[("wdint",)], writes=[("wdq", par)])

    def p3_mms(i):
        dblk, jcq = divmod(i, 16)
        par = i % 2
        for jcr in range(4):
            jc = jcq * 4 + jcr
            for dcc in range(4):
                ps = psA if dcc < 2 else psB
                pstok = ("psA",) if dcc < 2 else ("psB",)
                for th in range(2):
                    S.op("tensor", lambda t, ps=ps, dcc=dcc, th=th, jc=jc, par=par, jcr=jcr: t.matmul(
                        ps[:, (dcc % 2) * 1024 + th * 512:(dcc % 2) * 1024 + (th + 1) * 512],
                        wdq[:, par * 2048 + jcr * 512 + dcc * 128:par * 2048 + jcr * 512 + (dcc + 1) * 128],
                        q2res[:, jc * T + th * 512:jc * T + (th + 1) * 512],
                        start=(jc == 0), stop=(jc == 63)),
                        reads=[("wdq", par), ("q2res",)], writes=[pstok])

    def p3_epi(dblk):
        for dcc in range(4):
            ps = psA if dcc < 2 else psB
            pstok = ("psA",) if dcc < 2 else ("psB",)
            sl = dcc % 2
            S.op("vector", lambda v, ps=ps, dcc=dcc, sl=sl: v.tensor_mul(
                ffnst[:, sl * T:(sl + 1) * T], ps[:, (dcc % 2) * 1024:((dcc % 2) + 1) * 1024], gbc[:, :]),
                reads=[pstok, ("gbc",)], writes=[("ffnst", sl)])
            S.dma("gpsimd", ffnsc[dblk * 4 + dcc], ffnst[:, sl * T:(sl + 1) * T],
                  reads=[("ffnst", sl)], writes=[("ffnsc", dblk * 4 + dcc)])

    for i in range(66):
        if i < 64:
            p3_load(i)
        if 1 <= i <= 64:
            p3_tern(i - 1)
        if i >= 2:
            p3_mms(i - 2)
            if (i - 2) % 16 == 15:
                p3_epi((i - 2) // 16)

    S.barrier()
    ydmas = []
    for g in range(8):
        par = g % 2
        ps = psA if par == 0 else psB
        pstok = ("psA",) if par == 0 else ("psB",)
        av = accrd[:, par * D:(par + 1) * D]
        fv = ffrd[:, par * D:(par + 1) * D]
        yv = ystage[:, par * D:(par + 1) * D]
        S.dma("sync", av, accsc[g * 128:(g + 1) * 128, :],
              reads=[("accsc", g // 2, g % 2)], writes=[("accrd", par)])
        fsrc = ffnsc[:, :, g * 128:(g + 1) * 128].rearrange("c p t -> p c t")
        S.dma("sync", fv[:, :].rearrange("p (c t) -> p c t", c=16), fsrc,
              reads=[("ffnsc", dc) for dc in range(16)], writes=[("ffrd", par)])
        for dc in range(16):
            S.op("tensor", lambda t, dc=dc, ps=ps, fv=fv: t.transpose(
                ps[:, dc * 128:(dc + 1) * 128], fv[:, dc * 128:(dc + 1) * 128], ident[:, :]),
                reads=[("ffrd", par), CONST], writes=[pstok])
        S.op("vector", lambda v, ps=ps, yv=yv, av=av: v.tensor_add(yv, ps[:, :], av),
             reads=[pstok, ("accrd", par)], writes=[("ystage", par)])
        ydmas.append(("ysh", g))
        S.dma("gpsimd", ysh[g * 128:(g + 1) * 128, :], yv,
              reads=[("ystage", par)], writes=[("ysh", g)])
    if debug:
        S.dma("sync", dbgrow[0:1, :], m1row[0:1, :], reads=[("m1row",)], writes=[("dbg0",)])
        S.dma("sync", dbgrow[1:2, :], brow[0:1, :], reads=[("brow",)], writes=[("dbg1",)])
        S.dma("sync", dbgrow[2:3, :], grow[0:1, :], reads=[("grow",)], writes=[("dbg2",)])
        S.dma("sync", dbgrow[3:4, :], rbrow[0:1, :], reads=[("rbrow",)], writes=[("dbg3",)])
        ydmas = ydmas + [("dbgq1",), ("dbg0",), ("dbg1",), ("dbg2",), ("dbg3",)]
    S.op("sync", lambda e: e.nop(), reads=ydmas)

    from contextlib import ExitStack
    with ExitStack() as st:
        prog = {e: st.enter_context(nc.semaphore("pg_" + e)) for e in ENGINES}
        dqs = {e: st.enter_context(nc.semaphore("dq_" + e)) for e in ENGINES}
        ccsem = st.enter_context(nc.semaphore("ccsem"))
        block = st.enter_context(nc.Block())
        S.emit(nc, block, (prog, dqs, ccsem))
    return nc


_NC_CACHE = [None]


def kernel(L0_attn_x, L0_attn_B, L0_attn_C, L0_attn_D,
           L0_ffn_w_gate, L0_ffn_w_up, L0_ffn_w_down):
    x = np.ascontiguousarray(np.asarray(L0_attn_x, np.float32).reshape(8192, 2048))
    B = np.asarray(L0_attn_B, np.float32)
    C = np.asarray(L0_attn_C, np.float32)
    Dv = np.asarray(L0_attn_D, np.float32)
    wg = np.asarray(L0_ffn_w_gate, np.float32)
    wu = np.asarray(L0_ffn_w_up, np.float32)
    wd = np.asarray(L0_ffn_w_down, np.float32)

    bT = np.ascontiguousarray(B.T)
    cT = np.ascontiguousarray(C.T)
    dcolh = np.ascontiguousarray(Dv.reshape(16, 128).T)
    wgb = np.ascontiguousarray(wg.reshape(64, 128, 16, 128).transpose(0, 3, 2, 1)).reshape(64, 128, 2048)
    wub = np.ascontiguousarray(wu.reshape(64, 128, 16, 128).transpose(0, 3, 2, 1)).reshape(64, 128, 2048)
    pi = np.arange(128)
    gg = np.arange(64)
    jidx = (pi[:, None] // 2) * 128 + (pi[:, None] % 2) * 64 + gg[None, :]
    wdTp = np.ascontiguousarray(wd.T)[jidx.reshape(-1)].reshape(128, 64, 2048)
    wdb = np.ascontiguousarray(
        wdTp.reshape(128, 16, 4, 4, 4, 128).transpose(3, 1, 0, 2, 4, 5)).reshape(4, 16, 128, 2048)

    nsl = 8192 * 2048 // 8
    in_maps = []
    for c in range(8):
        wsl = np.ascontiguousarray(np.stack(
            [m.reshape(-1)[c * nsl:(c + 1) * nsl].reshape(128, 16384) for m in (wg, wu, wd)]))
        in_maps.append({
            "xsh": np.ascontiguousarray(x[c * 1024:(c + 1) * 1024]),
            "bT": bT, "cT": cT, "dcolh": dcolh,
            "wgb": wgb, "wub": wub, "wdb": wdb, "wsl": wsl,
        })
    if _NC_CACHE[0] is None:
        _NC_CACHE[0] = build()
    res = run_bass_kernel_spmd(_NC_CACHE[0], in_maps, core_ids=list(range(8)))
    y = np.concatenate([np.asarray(res.results[c]["ysh"]) for c in range(8)], axis=0)
    return y.reshape(2, 4096, 2048)


# version 13
# speedup vs baseline: 1.0555x; 1.0555x over previous
import numpy as np
import concourse.bass as bass
import concourse.mybir as mybir
from concourse.bass_utils import run_bass_kernel_spmd

F32 = mybir.dt.float32
BF16 = mybir.dt.bfloat16
I32 = mybir.dt.int32
U8 = mybir.dt.uint8
AF = mybir.ActivationFunctionType
ALU = mybir.AluOpType
AX = mybir.AxisListType

T = 1024
TT = 256
NTILE = T // TT
D = 2048
NST = 1024
K1_127 = float(np.float32(1.0 / np.sqrt(2048.0)) / np.float32(127.0))
K2_127 = float(np.float32(1.0 / np.sqrt(8192.0)) / np.float32(127.0))
E1 = float(np.float32(1e-5) * np.float32(np.sqrt(2048.0)))
E2C = float(np.float32(1e-5) * np.float32(np.sqrt(8192.0)))
WMEAN = 1.0 / 16777216.0

ENGINES = ("sync", "vector", "scalar", "tensor", "gpsimd")
SAME_ENGINE_SYNC = True


def _hadamard(n):
    h = np.array([[1.0]], dtype=np.float32)
    while h.shape[0] < n:
        h = np.block([[h, h], [h, -h]]).astype(np.float32)
    return h


class Arena:
    def __init__(self, nc, nbytes):
        self.nc = nc
        self.handle = nc.alloc_sbuf_tensor("arena", [128, nbytes], U8)
        self.base = nc.lookup_mloc(self.handle).addr

    def at(self, name, shape, dtype, off):
        nc = self.nc
        h = nc.virtual_sbuf_tensor(name, list(shape), dtype)
        nc._tensor(h.name, h.shape, h.dtype, type="SB")
        mloc = nc.lookup_mloc(h)
        mloc.allocated = not nc.target_bir_lowering
        mloc.addr = self.base + off
        return h


class Lay:
    def __init__(self, base):
        self.off = base

    def take(self, nbytes):
        o = (self.off + 31) // 32 * 32
        self.off = o + nbytes
        return o


class Sch:

    def __init__(self):
        self.ops = []
        self.last_w = {}
        self.readers = {}
        self.bar = set()

    def barrier(self):
        tails = {}
        for op in self.ops:
            tails[op["eng"]] = op
        self.bar = set()
        for op in tails.values():
            self.bar.add(id(op))
            self._byid[id(op)] = op

    def _deps(self, reads, writes):
        deps = set(self.bar)
        for t in reads:
            w = self.last_w.get(t)
            if w is not None:
                deps.add(id(w)); self._byid[id(w)] = w
        for t in writes:
            w = self.last_w.get(t)
            if w is not None:
                deps.add(id(w)); self._byid[id(w)] = w
            for r in self.readers.get(t, ()):
                deps.add(id(r)); self._byid[id(r)] = r
        return deps

    _byid = {}

    def _commit(self, op, reads, writes):
        for t in reads:
            self.readers.setdefault(t, []).append(op)
        for t in writes:
            self.last_w[t] = op
            self.readers[t] = []
        self.ops.append(op)

    def op(self, eng, emit, reads=(), writes=()):
        o = {"kind": "c", "eng": eng, "emit": emit,
             "deps": self._deps(reads, writes)}
        self._commit(o, reads, writes)
        return o

    def dma(self, eng, dst, src, reads=(), writes=()):
        o = {"kind": "d", "eng": eng, "dst": dst, "src": src,
             "deps": self._deps(reads, writes)}
        self._commit(o, reads, writes)
        return o

    def cc(self, emit, reads=(), writes=()):
        o = {"kind": "cc", "eng": "gpsimd", "emit": emit,
             "deps": self._deps(reads, writes)}
        self._commit(o, reads, writes)
        return o

    def emit(self, nc, block, sems):
        prog, dq, ccsem = sems
        qs = {e: [] for e in ENGINES}
        for op in self.ops:
            qs[op["eng"]].append(op)
        for e in ENGINES:
            dn = 0
            for op in qs[e]:
                if op["kind"] == "d":
                    op["dno"] = dn
                    dn += 1
        byid = Sch._byid
        for op in self.ops:
            for did in op["deps"]:
                d = byid[did]
                if d["kind"] == "d":
                    continue
                if d["eng"] != op["eng"]:
                    d["needs_inc"] = True
                elif SAME_ENGINE_SYNC and op["eng"] != "tensor":
                    d["needs_inc"] = True
        for e in ENGINES:
            m = 0
            for op in qs[e]:
                if op.get("needs_inc"):
                    m += 1
                    op["mno"] = m

        def mkbody(e):
            def body(eng):
                waited = {}
                for op in qs[e]:
                    wants = {}
                    for did in op["deps"]:
                        d = byid[did]
                        if d["kind"] == "d":
                            key = ("dq", d["eng"])
                            val = 16 * (d["dno"] + 1)
                        else:
                            if d["eng"] == e and not d.get("needs_inc"):
                                continue
                            if d["eng"] == e and not (SAME_ENGINE_SYNC and e != "tensor"):
                                continue
                            key = ("pg", d["eng"])
                            val = d["mno"]
                        if wants.get(key, 0) < val:
                            wants[key] = val
                    for key in sorted(wants, key=str):
                        val = wants[key]
                        if waited.get(key, 0) >= val:
                            continue
                        sem = dq[key[1]] if key[0] == "dq" else prog[key[1]]
                        eng.wait_ge(sem, val)
                        waited[key] = val
                    if op["kind"] == "c":
                        inst = op["emit"](eng)
                        if op.get("needs_inc"):
                            inst.then_inc(prog[e], 1)
                    elif op["kind"] == "d":
                        eng.dma_start(out=op["dst"], in_=op["src"]).then_inc(dq[e], 16)
                    else:
                        op["emit"](eng).then_inc(ccsem, 1)
                        eng.wait_ge(ccsem, 1)
                        if op.get("needs_inc"):
                            eng.nop().then_inc(prog[e], 1)
            return body

        for e in ENGINES:
            getattr(block, e)(mkbody(e))


def build(debug=False):
    nc = bass.Bass("TRN2", target_bir_lowering=False)
    SCR = "ExternalOutput" if debug else "Internal"

    xsh = nc.dram_tensor("xsh", [T, D], F32, kind="ExternalInput").ap()
    bTd = nc.dram_tensor("bT", [D, NST], F32, kind="ExternalInput").ap()
    cTd = nc.dram_tensor("cT", [NST, D], F32, kind="ExternalInput").ap()
    dcolh = nc.dram_tensor("dcolh", [128, 16], F32, kind="ExternalInput").ap()
    wgb = nc.dram_tensor("wgb", [64, 128, 2048], F32, kind="ExternalInput").ap()
    wub = nc.dram_tensor("wub", [64, 128, 2048], F32, kind="ExternalInput").ap()
    wdb = nc.dram_tensor("wdb", [4, 16, 128, 2048], F32, kind="ExternalInput").ap()
    wsl = nc.dram_tensor("wsl", [3, 128, 16384], F32, kind="ExternalInput").ap()
    ysh = nc.dram_tensor("ysh", [T, D], F32, kind="ExternalOutput").ap()

    hsc = nc.dram_tensor("hsc", [64, 128, T], F32, kind=SCR).ap()
    q2sc = nc.dram_tensor("q2sc", [NTILE, 128, 64 * TT], BF16, kind=SCR).ap()
    accsc = nc.dram_tensor("accsc", [T, D], F32, kind=SCR).ap()
    ffnsc = nc.dram_tensor("ffnsc", [16, 128, T], F32, kind=SCR).ap()
    if debug:
        dbgq1 = nc.dram_tensor("dbgq1", [128, 16 * T], BF16, kind="ExternalOutput").ap()
        dbgrow = nc.dram_tensor("dbgrow", [4, T], F32, kind="ExternalOutput").ap()
    ccin = nc.dram_tensor("ccin", [1, 4], F32, kind="Internal").ap()
    ccout = nc.dram_tensor("ccout", [1, 4], F32, kind="Internal").ap()

    h128c = nc.inline_tensor(_hadamard(128), name="h128c").ap()
    s2c = nc.inline_tensor(np.kron(_hadamard(64), np.eye(2, dtype=np.float32)).astype(np.float32),
                           name="s2c").ap()
    idenc = nc.inline_tensor(np.eye(128, dtype=np.float32), name="idenc").ap()
    onesrc = nc.inline_tensor(np.ones((1, 128), np.float32), name="onesrc").ap()
    onescc = nc.inline_tensor(np.ones((128, 1), np.float32), name="onescc").ap()

    AR = Arena(nc, 205824)
    P = Lay(0)
    hs = AR.at("hs", [128, 128], F32, P.take(512))
    s2s = AR.at("s2s", [128, 128], F32, P.take(512))
    ident = AR.at("ident", [128, 128], F32, P.take(512))
    ones_s = AR.at("ones_s", [1, 128], F32, P.take(512))
    onescol = AR.at("onescol", [128, 1], F32, P.take(32))
    dcol = AR.at("dcol", [128, 16], F32, P.take(64))
    rwb = AR.at("rwb", [128, 4], F32, P.take(32))
    scals = AR.at("scals", [1, 16], F32, P.take(64))
    colsc = AR.at("colsc", [128, 4], F32, P.take(16))
    rowtmp = AR.at("rowtmp", [2, 128], F32, P.take(512))
    m1row = AR.at("m1row", [1, T], F32, P.take(4096))
    brow = AR.at("brow", [1, T], F32, P.take(4096))
    rbrow = AR.at("rbrow", [1, T], F32, P.take(4096))
    grow = AR.at("grow", [1, T], F32, P.take(4096))
    rowsc = AR.at("rowsc", [1, 1024], F32, P.take(4096))
    bnb = AR.at("bnb", [128, 24], F32, P.take(96))
    bn2 = AR.at("bn2", [128, 2], F32, P.take(8))
    lnc = AR.at("lnc", [128, 8], F32, P.take(32))
    ccst = AR.at("ccst", [1, 4], F32, P.take(16))
    cols24 = AR.at("cols24", [128, 24], F32, P.take(96))
    col3 = AR.at("col3", [128, 4], F32, P.take(16))
    OB = (P.off + 31) // 32 * 32

    Q1_OFF = OB
    BT_OFF = OB + 32768
    CT_OFF = BT_OFF + 32768
    PRE_END = CT_OFF + 32768

    q1 = AR.at("q1", [128, 16 * T], BF16, Q1_OFF)
    BTb = AR.at("BTb", [128, 16 * NST], BF16, BT_OFF)
    CTb = AR.at("CTb", [128, 8 * D], BF16, CT_OFF)

    L0 = Lay(OB)
    wtmp0 = AR.at("wtmp0", [128, 2048], F32, L0.take(8192))
    wtmp1 = AR.at("wtmp1", [128, 2048], F32, L0.take(8192))
    wtmps = (wtmp0, wtmp1)

    L1 = Lay(PRE_END)
    xTM = AR.at("xTM", [128, 2 * D], F32, L1.take(16384))
    lnst = AR.at("lnst", [128, 2048], F32, L1.take(8192))
    xnF = AR.at("xnF", [128, 16 * TT], F32, L1.take(16384))
    Yf = AR.at("Yf", [128, 16 * TT], F32, L1.take(16384))
    xnB = AR.at("xnB", [128, 16 * TT], BF16, L1.take(8192))
    tanhS = AR.at("tanhS", [128, 8 * TT], BF16, L1.take(4096))
    accst = AR.at("accst", [128, D], F32, L1.take(8192))
    intst = AR.at("intst", [128, 512], I32, L1.take(2048))
    s1bc = AR.at("s1bc", [128, TT], F32, L1.take(1024))
    Mb = AR.at("Mb", [128, TT], F32, L1.take(1024))

    L15 = Lay(OB + 32768)
    wgraw = AR.at("wgraw", [128, 2 * 2048], F32, L15.take(16384))
    wuraw = AR.at("wuraw", [128, 2 * 2048], F32, L15.take(16384))
    wgq = AR.at("wgq", [128, 2 * 2048], BF16, L15.take(8192))
    wuq = AR.at("wuq", [128, 2 * 2048], BF16, L15.take(8192))
    wint = AR.at("wint", [128, 2048], I32, L15.take(8192))
    abc = AR.at("abc", [128, T], F32, L15.take(4096))
    tmpg = AR.at("tmpg", [128, T], F32, L15.take(4096))
    tmps = AR.at("tmps", [128, T], F32, L15.take(4096))
    hst = AR.at("hst", [128, T], F32, L15.take(4096))
    hstF = AR.at("hstF", [128, T], F32, L15.take(4096))

    L2 = Lay(OB)
    ping = AR.at("ping", [128, 64 * TT], F32, L2.take(65536))
    pong = AR.at("pong", [128, 64 * TT], F32, L2.take(65536))
    Mb2 = AR.at("Mb2", [128, TT], F32, L2.take(1024))
    s2bc = AR.at("s2bc", [128, TT], F32, L2.take(1024))
    q2st = AR.at("q2st", [128, 2 * 2048], BF16, L2.take(8192))
    i2st = AR.at("i2st", [128, 2 * 2048], I32, L2.take(16384))

    L3 = Lay(OB)
    q2res = AR.at("q2res", [128, 64 * T], BF16, L3.take(131072))
    wdraw = AR.at("wdraw", [128, 2 * 2048], F32, L3.take(16384))
    wdq = AR.at("wdq", [128, 2 * 2048], BF16, L3.take(8192))
    wdint = AR.at("wdint", [128, 2048], I32, L3.take(8192))
    gbc = AR.at("gbc", [128, T], F32, L3.take(4096))
    ffnst = AR.at("ffnst", [128, 2 * T], F32, L3.take(8192))

    L4 = Lay(OB)
    accrd = AR.at("accrd", [128, 2 * D], F32, L4.take(16384))
    ffrd = AR.at("ffrd", [128, 2 * D], F32, L4.take(16384))
    ystage = AR.at("ystage", [128, 2 * D], F32, L4.take(16384))

    psA = nc.alloc_psum_tensor("psA", [128, 2048], F32)
    psB = nc.alloc_psum_tensor("psB", [128, 2048], F32)

    S = Sch()
    CONST = ("const",)

    for g in range(2):
        S.dma("sync", xTM[:, g * D:(g + 1) * D], xsh[g * 128:(g + 1) * 128, :],
              writes=[("xTM", g)])
    S.dma("sync", hs[:, :], h128c, writes=[CONST])
    S.dma("sync", s2s[:, :], s2c, writes=[CONST])
    S.dma("sync", ident[:, :], idenc, writes=[CONST])
    S.dma("sync", ones_s[:, :], onesrc, writes=[CONST])
    S.dma("sync", onescol[:, :], onescc, writes=[CONST])
    S.dma("sync", dcol[:, :], dcolh, writes=[CONST])
    S.op("vector", lambda v: v.memset(scals[0:1, :], 0.0), writes=[("scals",)])
    S.op("vector", lambda v: v.memset(ccst[0:1, :], 0.0), writes=[("ccst",)])
    for m in range(3):
        for ch in range(8):
            i = m * 8 + ch
            par = i % 2
            S.dma("sync", wtmps[par][:, :], wsl[m, :, ch * 2048:(ch + 1) * 2048],
                  writes=[("wtmp", par)])
            S.op("vector", lambda v, i=i, par=par: v.tensor_reduce(
                cols24[:, i:i + 1], wtmps[par][:, :], AX.X, ALU.add,
                apply_absolute_value=True), reads=[("wtmp", par)], writes=[("c24", i)])
    for m in range(3):
        S.op("vector", lambda v, m=m: v.reduce_sum(
            col3[:, m:m + 1], cols24[:, m * 8:(m + 1) * 8], AX.X),
            reads=[("c24", m * 8 + j) for j in range(8)], writes=[("col3",)])
    S.op("tensor", lambda t: t.matmul(psA[0:1, 0:3], onescol[:, 0:1], col3[:, 0:3],
                                      start=True, stop=True),
         reads=[("col3",), CONST], writes=[("psA",)])
    S.op("vector", lambda v: v.tensor_copy(ccst[0:1, 0:3], psA[0:1, 0:3]),
         reads=[("psA",)], writes=[("ccst",)])
    S.dma("gpsimd", ccin, ccst[0:1, :], reads=[("ccst",)], writes=[("ccin",)])
    S.cc(lambda g: g.collective_compute(
        "AllReduce", ALU.add, replica_groups=[list(range(8))],
        ins=[ccin.opt()], outs=[ccout.opt()]),
        reads=[("ccin",)], writes=[("ccout",)])
    S.dma("gpsimd", ccst[0:1, :], ccout, reads=[("ccout",)], writes=[("ccst",)])
    S.op("vector", lambda v: v.tensor_scalar(scals[0:1, 0:3], ccst[0:1, 0:3],
                                             WMEAN, 1e-5, ALU.mult, ALU.max),
         reads=[("ccst",)], writes=[("scals",)])
    S.op("vector", lambda v: v.reciprocal(scals[0:1, 4:7], scals[0:1, 0:3]),
         reads=[("scals",)], writes=[("scals",)])
    S.op("vector", lambda v: v.tensor_single_scalar(scals[0:1, 8:9], scals[0:1, 0:1], K1_127, ALU.mult),
         reads=[("scals",)], writes=[("scals",)])
    S.op("vector", lambda v: v.tensor_single_scalar(scals[0:1, 9:10], scals[0:1, 1:2], K1_127, ALU.mult),
         reads=[("scals",)], writes=[("scals",)])
    S.op("vector", lambda v: v.tensor_single_scalar(scals[0:1, 10:11], scals[0:1, 2:3], K2_127, ALU.mult),
         reads=[("scals",)], writes=[("scals",)])
    S.op("tensor", lambda t: t.matmul(psA[0:128, 0:4], ones_s[0:1, 0:128], scals[0:1, 4:8],
                                      start=True, stop=True),
         reads=[("scals",), CONST], writes=[("psA",)])
    S.op("scalar", lambda s: s.copy(rwb[:, 0:4], psA[:, 0:4]),
         reads=[("psA",)], writes=[("rwb",)])
    for dc in range(16):
        par = dc % 2
        S.dma("sync", wtmps[par][:, 0:1024], bTd[dc * 128:(dc + 1) * 128, :], writes=[("wtmp", par)])
        S.op("vector", lambda v, dc=dc, par=par: v.tensor_copy(
            BTb[:, dc * 1024:(dc + 1) * 1024], wtmps[par][:, 0:1024]),
            reads=[("wtmp", par)], writes=[("BTb",)])
    for ncc in range(8):
        par = ncc % 2
        S.dma("sync", wtmps[par][:, :], cTd[ncc * 128:(ncc + 1) * 128, :], writes=[("wtmp", par)])
        S.op("vector", lambda v, ncc=ncc, par=par: v.tensor_copy(
            CTb[:, ncc * 2048:(ncc + 1) * 2048], wtmps[par][:, :]),
            reads=[("wtmp", par)], writes=[("CTb",)])

    xnFv = xnF[:, :].rearrange("p (c t) -> p c t", c=16)
    xnBv = xnB[:, :].rearrange("p (c t) -> p c t", c=16)
    psAv = psA[:, :].rearrange("p (c u) -> p c u", c=16)
    for tile in range(NTILE):
        t0 = tile * TT
        for g in range(2):
            if tile == 0:
                continue
            S.dma("sync", xTM[:, g * D:(g + 1) * D], xsh[t0 + g * 128:t0 + (g + 1) * 128, :],
                  writes=[("xTM", g)])
        for g in range(2):
            xg = xTM[:, g * D:(g + 1) * D]
            for i in range(4):
                S.op("vector", lambda v, xg=xg, i=i: v.bn_stats(
                    bnb[:, i * 6:(i + 1) * 6], xg[:, i * 512:(i + 1) * 512]),
                    reads=[("xTM", g)], writes=[("bnb",)])
            S.op("vector", lambda v: v.bn_aggr(bn2[:, 0:2], bnb[:, 0:24]),
                 reads=[("bnb",)], writes=[("bn2",)])
            S.op("vector", lambda v: v.tensor_single_scalar(lnc[:, 0:1], bn2[:, 1:2], 1e-5, ALU.add),
                 reads=[("bn2",)], writes=[("lnc",)])
            S.op("scalar", lambda s: s.activation(lnc[:, 1:2], lnc[:, 0:1], AF.Sqrt),
                 reads=[("lnc",)], writes=[("lnc",)])
            S.op("vector", lambda v: v.reciprocal(lnc[:, 2:3], lnc[:, 1:2]),
                 reads=[("lnc",)], writes=[("lnc",)])
            S.op("vector", lambda v: v.tensor_mul(lnc[:, 3:4], lnc[:, 2:3], lnc[:, 2:3]),
                 reads=[("lnc",)], writes=[("lnc",)])
            S.op("vector", lambda v: v.tensor_mul(lnc[:, 3:4], lnc[:, 3:4], lnc[:, 0:1]),
                 reads=[("lnc",)], writes=[("lnc",)])
            S.op("vector", lambda v: v.tensor_scalar(lnc[:, 3:4], lnc[:, 3:4], -0.5, 1.5, ALU.mult, ALU.add),
                 reads=[("lnc",)], writes=[("lnc",)])
            S.op("vector", lambda v: v.tensor_mul(lnc[:, 2:3], lnc[:, 2:3], lnc[:, 3:4]),
                 reads=[("lnc",)], writes=[("lnc",)])
            S.op("vector", lambda v: v.scalar_tensor_tensor(
                lnc[:, 4:5], bn2[:, 0:1], -1.0, lnc[:, 2:3], ALU.mult, ALU.mult),
                reads=[("lnc",), ("bn2",)], writes=[("lnc",)])
            psg = psA if g == 0 else psB
            ptokg = ("psA",) if g == 0 else ("psB",)
            psgv = psg[:, :].rearrange("p (c u) -> p c u", c=16)
            S.op("scalar", lambda s, xg=xg: s.activation(
                lnst[:, :], xg, AF.Identity, bias=lnc[:, 4:5], scale=lnc[:, 2:3]),
                reads=[("xTM", g), ("lnc",)], writes=[("lnst",)])
            for dc in range(16):
                S.op("tensor", lambda t, dc=dc, psg=psg: t.transpose(
                    psg[:, dc * 128:(dc + 1) * 128], lnst[:, dc * 128:(dc + 1) * 128], ident[:, :]),
                    reads=[("lnst",), CONST], writes=[ptokg])
            S.op("vector", lambda v, g=g, psgv=psgv: v.tensor_copy(
                xnFv[:, :, g * 128:(g + 1) * 128], psgv),
                reads=[ptokg], writes=[("xnF",)])
            S.op("scalar", lambda s, g=g: s.copy(
                xnBv[:, :, g * 128:(g + 1) * 128], xnFv[:, :, g * 128:(g + 1) * 128]),
                reads=[("xnF",)], writes=[("xnB",)])
        for ncc in range(8):
            for dc in range(16):
                S.op("tensor", lambda t, ncc=ncc, dc=dc: t.matmul(
                    psA[:, ncc * TT:(ncc + 1) * TT],
                    BTb[:, dc * 1024 + ncc * 128: dc * 1024 + (ncc + 1) * 128],
                    xnB[:, dc * TT:(dc + 1) * TT],
                    start=(dc == 0), stop=(dc == 15)),
                    reads=[("xnB",), ("BTb",)], writes=[("psA",)])
        S.op("scalar", lambda s: s.activation(tanhS[:, :], psA[:, :], AF.Tanh),
             reads=[("psA",)], writes=[("tanhS",)])
        for wv in range(2):
            psw = psB if wv == 0 else psA
            ptokw = ("psB",) if wv == 0 else ("psA",)
            for dc in range(wv * 8, wv * 8 + 8):
                for ncc in range(8):
                    S.op("tensor", lambda t, dc=dc, ncc=ncc, psw=psw: t.matmul(
                        psw[:, (dc % 8) * TT:((dc % 8) + 1) * TT],
                        CTb[:, ncc * 2048 + dc * 128: ncc * 2048 + (dc + 1) * 128],
                        tanhS[:, ncc * TT:(ncc + 1) * TT],
                        start=(ncc == 0), stop=(ncc == 7)),
                        reads=[("tanhS",), ("CTb",)], writes=[ptokw])
            for dc in range(wv * 8, wv * 8 + 8):
                S.op("vector", lambda v, dc=dc, psw=psw: v.scalar_tensor_tensor(
                    Yf[:, dc * TT:(dc + 1) * TT], xnF[:, dc * TT:(dc + 1) * TT],
                    dcol[:, dc:dc + 1], psw[:, (dc % 8) * TT:((dc % 8) + 1) * TT],
                    ALU.mult, ALU.add),
                    reads=[("xnF",), ptokw, CONST], writes=[("Yf",)])
        for g in range(2):
            for dc in range(16):
                S.op("tensor", lambda t, dc=dc, g=g: t.transpose(
                    psA[:, dc * 128:(dc + 1) * 128],
                    Yf[:, dc * TT + g * 128: dc * TT + (g + 1) * 128], ident[:, :]),
                    reads=[("Yf",), CONST], writes=[("psA",)])
            S.op("vector", lambda v, g=g: v.tensor_add(
                accst[:, :], psA[:, :], xTM[:, g * D:(g + 1) * D]),
                reads=[("psA",), ("xTM", g)], writes=[("accst",)])
            S.dma("gpsimd", accsc[t0 + g * 128:t0 + (g + 1) * 128, :], accst[:, :],
                  reads=[("accst",)], writes=[("accsc", tile, g)])
        for wv in range(2):
            psw = psB if wv == 0 else psA
            ptokw = ("psB",) if wv == 0 else ("psA",)
            for c in range(wv * 8, wv * 8 + 8):
                S.op("tensor", lambda t, c=c, psw=psw: t.matmul(
                    psw[:, (c % 8) * TT:((c % 8) + 1) * TT], hs[:, :],
                    xnF[:, c * TT:(c + 1) * TT], start=True, stop=True),
                    reads=[("xnF",), CONST], writes=[ptokw])
            S.op("scalar", lambda s, wv=wv, psw=psw: s.copy(
                Yf[:, wv * 2048:(wv + 1) * 2048], psw[:, :]),
                reads=[ptokw], writes=[("Yf",)])
        bufs = [(Yf, ("Yf",)), (xnF, ("xnF",))]
        for st in range(4):
            h = 1 << st
            src, stok = bufs[st % 2]
            dst, dtok = bufs[(st + 1) % 2]
            va = src[:, :].rearrange("p (a two rest) -> p a two rest", two=2, rest=h * TT)
            vb = dst[:, :].rearrange("p (a two rest) -> p a two rest", two=2, rest=h * TT)
            S.op("vector", lambda v, va=va, vb=vb: v.tensor_tensor(
                vb[:, :, 0, :], va[:, :, 0, :], va[:, :, 1, :], ALU.add),
                reads=[stok], writes=[dtok])
            S.op("vector", lambda v, va=va, vb=vb: v.tensor_tensor(
                vb[:, :, 1, :], va[:, :, 0, :], va[:, :, 1, :], ALU.subtract),
                reads=[stok], writes=[dtok])
        Yred = Yf[:, :].rearrange("p (c t) -> p t c", c=16)
        S.op("vector", lambda v, Yred=Yred: v.reduce_max(
            Mb[:, 0:TT][:, :, None], Yred, AX.X, apply_absolute_value=True),
            reads=[("Yf",)], writes=[("Mb",)])
        for g in range(2):
            S.op("tensor", lambda t, g=g: t.transpose(
                psB[:, g * 128:(g + 1) * 128], Mb[:, g * 128:(g + 1) * 128], ident[:, :]),
                reads=[("Mb",), CONST], writes=[("psB",)])
        pv = psB[:, 0:256].rearrange("p (g f) -> p g f", g=2)
        S.op("vector", lambda v, pv=pv: v.reduce_max(colsc[:, 0:2][:, :, None], pv, AX.X),
             reads=[("psB",)], writes=[("colsc",)])
        S.op("vector", lambda v: v.tensor_single_scalar(colsc[:, 0:2], colsc[:, 0:2], E1, ALU.max),
             reads=[("colsc",)], writes=[("colsc",)])
        S.op("tensor", lambda t: t.transpose(psB[0:2, 256:384], colsc[:, 0:2], ident[:, :]),
             reads=[("colsc",), CONST], writes=[("psB",)])
        S.op("vector", lambda v: v.tensor_copy(rowtmp[0:2, :], psB[0:2, 256:384]),
             reads=[("psB",)], writes=[("rowtmp",)])
        S.op("vector", lambda v, t0=t0: v.tensor_copy(m1row[0:1, t0:t0 + 128], rowtmp[0:1, :]),
             reads=[("rowtmp",)], writes=[("m1row",)])
        S.dma("sync", m1row[0:1, t0 + 128:t0 + 256], rowtmp[1:2, :],
              reads=[("rowtmp",)], writes=[("m1row",)])
        S.op("vector", lambda v, t0=t0: v.reciprocal(rowsc[0:1, 0:TT], m1row[0:1, t0:t0 + TT]),
             reads=[("m1row",)], writes=[("rowsc",)])
        S.op("vector", lambda v: v.tensor_single_scalar(rowsc[0:1, 256:256 + TT],
                                                        rowsc[0:1, 0:TT], 127.0, ALU.mult),
             reads=[("rowsc",)], writes=[("rowsc",)])
        S.op("tensor", lambda t: t.matmul(psB[0:128, 0:TT], ones_s[0:1, 0:128],
                                          rowsc[0:1, 256:256 + TT], start=True, stop=True),
             reads=[("rowsc",), CONST], writes=[("psB",)])
        S.op("scalar", lambda s: s.copy(s1bc[:, :], psB[:, 0:TT]),
             reads=[("psB",)], writes=[("s1bc",)])
        for c in range(16):
            par = c % 2
            S.op("vector", lambda v, c=c, par=par: v.tensor_tensor(
                intst[:, par * TT:(par + 1) * TT], Yf[:, c * TT:(c + 1) * TT], s1bc[:, :], ALU.mult),
                reads=[("Yf",), ("s1bc",)], writes=[("intst", par)])
            extra = [("wtmp", 0), ("wtmp", 1)] if (tile == 0 and c == 0) else []
            S.op("vector", lambda v, c=c, par=par, t0=t0: v.tensor_copy(
                q1[:, c * T + t0:c * T + t0 + TT], intst[:, par * TT:(par + 1) * TT]),
                reads=[("intst", par)], writes=[("q1",)] + extra)

    S.barrier()
    S.op("vector", lambda v: v.tensor_scalar(rowsc[0:1, 0:1024], m1row[0:1, :],
                                             scals[0:1, 8:9], None, ALU.mult),
         reads=[("m1row",), ("scals",)], writes=[("rowsc",)])
    S.op("vector", lambda v: v.tensor_scalar(brow[0:1, :], m1row[0:1, :],
                                             scals[0:1, 9:10], None, ALU.mult),
         reads=[("m1row",), ("scals",)], writes=[("brow",)])
    for hh in range(2):
        S.op("tensor", lambda t, hh=hh: t.matmul(
            psA[0:128, hh * 512:(hh + 1) * 512], ones_s[0:1, 0:128],
            rowsc[0:1, hh * 512:(hh + 1) * 512], start=True, stop=True),
            reads=[("rowsc",), CONST], writes=[("psA",)])
    S.op("scalar", lambda s: s.copy(abc[:, :], psA[:, 0:1024]),
         reads=[("psA",)], writes=[("abc",)])

    def p15_load(mc):
        par = mc % 2
        S.dma("sync", wgraw[:, par * 2048:(par + 1) * 2048], wgb[mc], writes=[("wgraw", par)])
        S.dma("sync", wuraw[:, par * 2048:(par + 1) * 2048], wub[mc], writes=[("wuraw", par)])

    def p15_tern(mc):
        par = mc % 2
        for nm, raw, qq, ri in (("g", wgraw, wgq, 0), ("u", wuraw, wuq, 1)):
            rv = raw[:, par * 2048:(par + 1) * 2048]
            qv = qq[:, par * 2048:(par + 1) * 2048]
            rtok = ("w%sraw" % nm, par)
            qtok = ("w%sq" % nm, par)
            S.op("vector", lambda v, rv=rv, ri=ri: v.tensor_scalar(
                wint[:, :], rv, rwb[:, ri:ri + 1], -1.49, ALU.mult, ALU.max),
                reads=[rtok, ("rwb",)], writes=[("wint",)])
            S.op("vector", lambda v, qv=qv: v.tensor_scalar(
                qv, wint[:, :], 1.0, None, ALU.min),
                reads=[("wint",)], writes=[qtok])

    def p15_G(mc):
        par = mc % 2
        ps = psA if par == 0 else psB
        gtok = ("psG", par)
        for kc in range(16):
            for th in range(2):
                S.op("tensor", lambda t, kc=kc, th=th, ps=ps, par=par: t.matmul(
                    ps[:, th * 512:(th + 1) * 512],
                    wgq[:, par * 2048 + kc * 128:par * 2048 + (kc + 1) * 128],
                    q1[:, kc * T + th * 512:kc * T + (th + 1) * 512],
                    start=(kc == 0), stop=(kc == 15)),
                    reads=[("wgq", par), ("q1",)], writes=[gtok])

    def p15_U(mc):
        par = mc % 2
        ps = psA if par == 0 else psB
        utok = ("psU", par)
        for kc in range(16):
            for th in range(2):
                S.op("tensor", lambda t, kc=kc, th=th, ps=ps, par=par: t.matmul(
                    ps[:, 1024 + th * 512:1024 + (th + 1) * 512],
                    wuq[:, par * 2048 + kc * 128:par * 2048 + (kc + 1) * 128],
                    q1[:, kc * T + th * 512:kc * T + (th + 1) * 512],
                    start=(kc == 0), stop=(kc == 15)),
                    reads=[("wuq", par), ("q1",)], writes=[utok])

    def p15_epi(mc):
        par = mc % 2
        ps = psA if par == 0 else psB
        gtok = ("psG", par)
        utok = ("psU", par)
        S.op("vector", lambda v, ps=ps: v.tensor_mul(tmpg[:, :], ps[:, 0:1024], abc[:, :]),
             reads=[gtok, ("abc",)], writes=[("tmpg",)])
        S.op("scalar", lambda s: s.activation(tmps[:, :], tmpg[:, :], AF.Silu),
             reads=[("tmpg",)], writes=[("tmps",)])
        S.op("vector", lambda v, ps=ps: v.tensor_mul(hst[:, :], tmps[:, :], ps[:, 1024:2048]),
             reads=[("tmps",), utok], writes=[("hst",)])

    def p15_mmA(mc):
        par = mc % 2
        ps = psA if par == 0 else psB
        gtok = ("psG", par)
        for th in range(2):
            S.op("tensor", lambda t, th=th, ps=ps: t.matmul(
                ps[:, th * 512:(th + 1) * 512], hs[:, :],
                hst[:, th * 512:(th + 1) * 512], start=True, stop=True),
                reads=[("hst",), CONST], writes=[gtok])
        S.op("scalar", lambda s, ps=ps: s.copy(hstF[:, :], ps[:, 0:1024]),
             reads=[gtok], writes=[("hstF",)])
        S.dma("gpsimd", hsc[mc], hstF[:, :], reads=[("hstF",)], writes=[("hsc", mc)])

    for i in range(67):
        if i < 64:
            p15_load(i)
        if 1 <= i <= 64:
            p15_tern(i - 1)
        if 2 <= i <= 65:
            mc = i - 2
            p15_G(mc)
            if mc >= 1:
                p15_mmA(mc - 1)
            p15_U(mc)
            p15_epi(mc)
        if i == 66:
            p15_mmA(63)

    if debug:
        S.dma("sync", dbgq1, q1[:, :], reads=[("q1",)], writes=[("dbgq1",)])

    S.barrier()
    S.op("vector", lambda v: v.reciprocal(rbrow[0:1, :], brow[0:1, :]),
         reads=[("brow",)], writes=[("rbrow",)])
    allhsc = [("hsc", c) for c in range(64)]
    hscr = hsc.rearrange("c (b g) t -> (c b) g t", b=2)
    for tile in range(NTILE):
        t0 = tile * TT
        pv3 = ping[:, :].rearrange("p (c t) -> p c t", c=64)
        for k in range(4):
            S.dma("sync", pv3[:, 16 * k:16 * (k + 1), :],
                  hscr[:, 16 * k:16 * (k + 1), t0:t0 + TT],
                  reads=allhsc, writes=[("ping",)])
        for wv in range(16):
            half = wv % 2
            htok = ("psAh", half)
            for j in range(4):
                gm = wv * 4 + j
                S.op("tensor", lambda t, gm=gm, j=j, half=half: t.matmul(
                    psA[:, half * 1024 + j * TT:half * 1024 + (j + 1) * TT], s2s[:, :],
                    ping[:, gm * TT:(gm + 1) * TT], start=True, stop=True),
                    reads=[("ping",), CONST], writes=[htok])
            S.op("scalar", lambda s, wv=wv, half=half: s.copy(
                pong[:, wv * 1024:(wv + 1) * 1024], psA[:, half * 1024:(half + 1) * 1024]),
                reads=[htok], writes=[("pong", wv)])
        allpong = [("pong", wv) for wv in range(16)]
        Pred = pong[:, :].rearrange("p (c t) -> p t c", c=64)
        S.op("vector", lambda v, Pred=Pred: v.reduce_max(
            Mb2[:, 0:TT][:, :, None], Pred, AX.X, apply_absolute_value=True),
            reads=allpong, writes=[("Mb2",)])
        for g in range(2):
            S.op("tensor", lambda t, g=g: t.transpose(
                psB[:, g * 128:(g + 1) * 128], Mb2[:, g * 128:(g + 1) * 128], ident[:, :]),
                reads=[("Mb2",), CONST], writes=[("psB",)])
        pv = psB[:, 0:256].rearrange("p (g f) -> p g f", g=2)
        S.op("vector", lambda v, pv=pv: v.reduce_max(colsc[:, 0:2][:, :, None], pv, AX.X),
             reads=[("psB",)], writes=[("colsc",)])
        S.op("tensor", lambda t: t.transpose(psB[0:2, 256:384], colsc[:, 0:2], ident[:, :]),
             reads=[("colsc",), CONST], writes=[("psB",)])
        S.op("vector", lambda v: v.tensor_copy(rowtmp[0:2, :], psB[0:2, 256:384]),
             reads=[("psB",)], writes=[("rowtmp",)])
        S.op("vector", lambda v: v.tensor_copy(rowsc[0:1, 0:128], rowtmp[0:1, :]),
             reads=[("rowtmp",)], writes=[("rowsc",)])
        S.dma("sync", rowsc[0:1, 128:256], rowtmp[1:2, :],
              reads=[("rowtmp",)], writes=[("rowsc",)])
        S.op("vector", lambda v, t0=t0: v.scalar_tensor_tensor(
            rowsc[0:1, 256:256 + TT], rbrow[0:1, t0:t0 + TT], E2C,
            rowsc[0:1, 0:TT], ALU.mult, ALU.max),
            reads=[("rowsc",), ("rbrow",)], writes=[("rowsc",)])
        S.op("vector", lambda v: v.reciprocal(rowsc[0:1, 512:512 + TT], rowsc[0:1, 256:256 + TT]),
             reads=[("rowsc",)], writes=[("rowsc",)])
        S.op("vector", lambda v: v.tensor_single_scalar(
            rowsc[0:1, 768:768 + TT], rowsc[0:1, 512:512 + TT], 127.0, ALU.mult),
            reads=[("rowsc",)], writes=[("rowsc",)])
        S.op("vector", lambda v, t0=t0: v.scalar_tensor_tensor(
            grow[0:1, t0:t0 + TT], brow[0:1, t0:t0 + TT], scals[0:1, 10:11],
            rowsc[0:1, 256:256 + TT], ALU.mult, ALU.mult),
            reads=[("brow",), ("scals",), ("rowsc",)], writes=[("grow",)])
        S.op("tensor", lambda t: t.matmul(psB[0:128, 0:TT], ones_s[0:1, 0:128],
                                          rowsc[0:1, 768:768 + TT], start=True, stop=True),
             reads=[("rowsc",), CONST], writes=[("psB",)])
        S.op("scalar", lambda s: s.copy(s2bc[:, :], psB[:, 0:TT]),
             reads=[("psB",)], writes=[("s2bc",)])
        for grp in range(8):
            par = grp % 2
            pview = pong[:, :].rearrange("p (c t) -> p c t", c=64)[:, grp * 8:(grp + 1) * 8, :]
            sview = s2bc[:, None, :].broadcast_to((128, 8, TT))
            iview = i2st[:, par * 2048:(par + 1) * 2048].rearrange("p (c t) -> p c t", c=8)
            S.op("vector", lambda v, pview=pview, sview=sview, iview=iview: v.tensor_tensor(
                iview, pview, sview, ALU.mult),
                reads=allpong + [("s2bc",)], writes=[("i2st", par)])
            S.op("vector", lambda v, par=par: v.tensor_copy(
                q2st[:, par * 2048:(par + 1) * 2048], i2st[:, par * 2048:(par + 1) * 2048]),
                reads=[("i2st", par)], writes=[("q2st", par)])
            S.dma("gpsimd", q2sc[tile, :, grp * 2048:(grp + 1) * 2048],
                  q2st[:, par * 2048:(par + 1) * 2048],
                  reads=[("q2st", par)], writes=[("q2sc", tile, grp)])

    S.barrier()
    for hh in range(2):
        S.op("tensor", lambda t, hh=hh: t.matmul(
            psA[0:128, hh * 512:(hh + 1) * 512], ones_s[0:1, 0:128],
            grow[0:1, hh * 512:(hh + 1) * 512], start=True, stop=True),
            reads=[("grow",), CONST], writes=[("psA",)])
    S.op("scalar", lambda s: s.copy(gbc[:, :], psA[:, 0:1024]),
         reads=[("psA",)], writes=[("gbc",)])
    q2v = q2res[:, :].rearrange("p (c tk) -> p c tk", c=64)
    for tl in range(NTILE):
        src = q2sc[tl].rearrange("p (c t) -> p c t", c=64)
        S.dma("gpsimd", q2v[:, :, tl * TT:(tl + 1) * TT], src,
              reads=[("q2sc", tl, grp) for grp in range(8)], writes=[("q2res",)])

    def p3_load(i):
        dblk, jcq = divmod(i, 16)
        par = i % 2
        S.dma("sync", wdraw[:, par * 2048:(par + 1) * 2048], wdb[dblk, jcq], writes=[("wdraw", par)])

    def p3_tern(i):
        par = i % 2
        rv = wdraw[:, par * 2048:(par + 1) * 2048]
        qv = wdq[:, par * 2048:(par + 1) * 2048]
        S.op("vector", lambda v, rv=rv: v.tensor_scalar(
            wdint[:, :], rv, rwb[:, 2:3], -1.49, ALU.mult, ALU.max),
            reads=[("wdraw", par), ("rwb",)], writes=[("wdint",)])
        S.op("vector", lambda v, qv=qv: v.tensor_scalar(
            qv, wdint[:, :], 1.0, None, ALU.min),
            reads=[("wdint",)], writes=[("wdq", par)])

    def p3_mms(i):
        dblk, jcq = divmod(i, 16)
        par = i % 2
        for jcr in range(4):
            jc = jcq * 4 + jcr
            for dcc in range(4):
                ps = psA if dcc < 2 else psB
                pstok = ("psA",) if dcc < 2 else ("psB",)
                for th in range(2):
                    S.op("tensor", lambda t, ps=ps, dcc=dcc, th=th, jc=jc, par=par, jcr=jcr: t.matmul(
                        ps[:, (dcc % 2) * 1024 + th * 512:(dcc % 2) * 1024 + (th + 1) * 512],
                        wdq[:, par * 2048 + jcr * 512 + dcc * 128:par * 2048 + jcr * 512 + (dcc + 1) * 128],
                        q2res[:, jc * T + th * 512:jc * T + (th + 1) * 512],
                        start=(jc == 0), stop=(jc == 63)),
                        reads=[("wdq", par), ("q2res",)], writes=[pstok])

    def p3_epi(dblk):
        for dcc in range(4):
            ps = psA if dcc < 2 else psB
            pstok = ("psA",) if dcc < 2 else ("psB",)
            sl = dcc % 2
            S.op("vector", lambda v, ps=ps, dcc=dcc, sl=sl: v.tensor_mul(
                ffnst[:, sl * T:(sl + 1) * T], ps[:, (dcc % 2) * 1024:((dcc % 2) + 1) * 1024], gbc[:, :]),
                reads=[pstok, ("gbc",)], writes=[("ffnst", sl)])
            S.dma("gpsimd", ffnsc[dblk * 4 + dcc], ffnst[:, sl * T:(sl + 1) * T],
                  reads=[("ffnst", sl)], writes=[("ffnsc", dblk * 4 + dcc)])

    for i in range(66):
        if i < 64:
            p3_load(i)
        if 1 <= i <= 64:
            p3_tern(i - 1)
        if i >= 2:
            p3_mms(i - 2)
            if (i - 2) % 16 == 15:
                p3_epi((i - 2) // 16)

    S.barrier()
    ydmas = []
    for g in range(8):
        par = g % 2
        ps = psA if par == 0 else psB
        pstok = ("psA",) if par == 0 else ("psB",)
        av = accrd[:, par * D:(par + 1) * D]
        fv = ffrd[:, par * D:(par + 1) * D]
        yv = ystage[:, par * D:(par + 1) * D]
        S.dma("sync", av, accsc[g * 128:(g + 1) * 128, :],
              reads=[("accsc", g // 2, g % 2)], writes=[("accrd", par)])
        fsrc = ffnsc[:, :, g * 128:(g + 1) * 128].rearrange("c p t -> p c t")
        S.dma("sync", fv[:, :].rearrange("p (c t) -> p c t", c=16), fsrc,
              reads=[("ffnsc", dc) for dc in range(16)], writes=[("ffrd", par)])
        for dc in range(16):
            S.op("tensor", lambda t, dc=dc, ps=ps, fv=fv: t.transpose(
                ps[:, dc * 128:(dc + 1) * 128], fv[:, dc * 128:(dc + 1) * 128], ident[:, :]),
                reads=[("ffrd", par), CONST], writes=[pstok])
        S.op("vector", lambda v, ps=ps, yv=yv, av=av: v.tensor_add(yv, ps[:, :], av),
             reads=[pstok, ("accrd", par)], writes=[("ystage", par)])
        ydmas.append(("ysh", g))
        S.dma("gpsimd", ysh[g * 128:(g + 1) * 128, :], yv,
              reads=[("ystage", par)], writes=[("ysh", g)])
    if debug:
        S.dma("sync", dbgrow[0:1, :], m1row[0:1, :], reads=[("m1row",)], writes=[("dbg0",)])
        S.dma("sync", dbgrow[1:2, :], brow[0:1, :], reads=[("brow",)], writes=[("dbg1",)])
        S.dma("sync", dbgrow[2:3, :], grow[0:1, :], reads=[("grow",)], writes=[("dbg2",)])
        S.dma("sync", dbgrow[3:4, :], rbrow[0:1, :], reads=[("rbrow",)], writes=[("dbg3",)])
        ydmas = ydmas + [("dbgq1",), ("dbg0",), ("dbg1",), ("dbg2",), ("dbg3",)]
    S.op("sync", lambda e: e.nop(), reads=ydmas)

    from contextlib import ExitStack
    with ExitStack() as st:
        prog = {e: st.enter_context(nc.semaphore("pg_" + e)) for e in ENGINES}
        dqs = {e: st.enter_context(nc.semaphore("dq_" + e)) for e in ENGINES}
        ccsem = st.enter_context(nc.semaphore("ccsem"))
        block = st.enter_context(nc.Block())
        S.emit(nc, block, (prog, dqs, ccsem))
    return nc


_NC_CACHE = [None]


def kernel(L0_attn_x, L0_attn_B, L0_attn_C, L0_attn_D,
           L0_ffn_w_gate, L0_ffn_w_up, L0_ffn_w_down):
    x = np.ascontiguousarray(np.asarray(L0_attn_x, np.float32).reshape(8192, 2048))
    B = np.asarray(L0_attn_B, np.float32)
    C = np.asarray(L0_attn_C, np.float32)
    Dv = np.asarray(L0_attn_D, np.float32)
    wg = np.asarray(L0_ffn_w_gate, np.float32)
    wu = np.asarray(L0_ffn_w_up, np.float32)
    wd = np.asarray(L0_ffn_w_down, np.float32)

    bT = np.ascontiguousarray(B.T)
    cT = np.ascontiguousarray(C.T)
    dcolh = np.ascontiguousarray(Dv.reshape(16, 128).T)
    wgb = np.ascontiguousarray(wg.reshape(64, 128, 16, 128).transpose(0, 3, 2, 1)).reshape(64, 128, 2048)
    wub = np.ascontiguousarray(wu.reshape(64, 128, 16, 128).transpose(0, 3, 2, 1)).reshape(64, 128, 2048)
    pi = np.arange(128)
    gg = np.arange(64)
    jidx = (pi[:, None] // 2) * 128 + (pi[:, None] % 2) * 64 + gg[None, :]
    wdTp = np.ascontiguousarray(wd.T)[jidx.reshape(-1)].reshape(128, 64, 2048)
    wdb = np.ascontiguousarray(
        wdTp.reshape(128, 16, 4, 4, 4, 128).transpose(3, 1, 0, 2, 4, 5)).reshape(4, 16, 128, 2048)

    nsl = 8192 * 2048 // 8
    in_maps = []
    for c in range(8):
        wsl = np.ascontiguousarray(np.stack(
            [m.reshape(-1)[c * nsl:(c + 1) * nsl].reshape(128, 16384) for m in (wg, wu, wd)]))
        in_maps.append({
            "xsh": np.ascontiguousarray(x[c * 1024:(c + 1) * 1024]),
            "bT": bT, "cT": cT, "dcolh": dcolh,
            "wgb": wgb, "wub": wub, "wdb": wdb, "wsl": wsl,
        })
    if _NC_CACHE[0] is None:
        _NC_CACHE[0] = build()
    res = run_bass_kernel_spmd(_NC_CACHE[0], in_maps, core_ids=list(range(8)))
    y = np.concatenate([np.asarray(res.results[c]["ysh"]) for c in range(8)], axis=0)
    return y.reshape(2, 4096, 2048)
